# version 02
# baseline (speedup 1.0000x reference)
import os
os.environ.setdefault("GAUGE_NTFF_IGNORE_DMA", "1")
import numpy as np
import concourse.bass as bass
import concourse.mybir as mybir
import concourse.bass_utils as _bu

if not getattr(_bu, "_no_birverifier", False):
    _orig_run_command = _bu.run_command

    def _run_command_no_verify(argv, **kw):
        argv = [a.replace("birverifier,", "") if isinstance(a, str) else a
                for a in argv]
        return _orig_run_command(argv, **kw)

    _bu.run_command = _run_command_no_verify
    _bu._no_birverifier = True

F32 = mybir.dt.float32
F32R = mybir.dt.float32r
BF16 = mybir.dt.bfloat16
AF = mybir.ActivationFunctionType
ALU = mybir.AluOpType
AX = mybir.AxisListType

D = 1024
H = 4096
NB = 8
HB = 32
S = 512
FS = 256
EPS_LN = 1e-5
EPS_WKV = 1e-9
CLAMP = float(np.exp(np.float64(-30.0)))
I1024 = 1.0 / 1024.0


class Sched:
    ENGS = [("sync", "d"), ("gpsimd", "d"), ("gpsimd", "cc"), ("gpsimd", "c"),
            ("scalar", "d"), ("scalar", "c"), ("vector", "c"), ("tensor", "c")]

    def __init__(self):
        self.steps = []

    def add(self, engine, klass, emit, deps=(), n_dma=0, tag=""):
        sid = len(self.steps)
        self.steps.append(dict(engine=engine, klass=klass, emit=emit,
                               deps=[d for d in deps if d is not None],
                               n_dma=n_dma, tag=tag))
        return sid

    def dma(self, engine, emit, deps=(), n=1, tag=""):
        return self.add(engine, "d", emit, deps, n_dma=n, tag=tag)

    def comp(self, engine, emit, deps=(), tag=""):
        return self.add(engine, "c", emit, deps, tag=tag)

    def dep(self, sid, extra):
        if extra is not None:
            self.steps[sid]["deps"].append(extra)

    def assign_counts(self):
        cum = {k: 0 for k in self.ENGS}
        for st in self.steps:
            key = (st["engine"], st["klass"])
            cum[key] += 16 * st["n_dma"] if st["klass"] == "d" else 1
            st["post"] = dict(cum)
            st["key"] = key

    def emit_engine(self, engine, eng, sems, limit=None):
        waited = {}
        for sid, st in enumerate(self.steps):
            if limit is not None and sid >= limit:
                break
            if st["engine"] != engine:
                continue
            need = {}
            for d in st["deps"]:
                dep = self.steps[d]
                k = dep["key"]
                if k == st["key"]:
                    continue
                v = dep["post"][k]
                if need.get(k, 0) < v:
                    need[k] = v
            for k, v in sorted(need.items(), key=lambda kv: (kv[0][0], kv[0][1])):
                if waited.get(k, 0) >= v:
                    continue
                eng.wait_ge(sems[k], v)
                waited[k] = v
            instrs = st["emit"](eng)
            if instrs is None:
                instrs = []
            if not isinstance(instrs, (list, tuple)):
                instrs = [instrs]
            if st["klass"] == "d":
                assert len(instrs) == st["n_dma"], (st["tag"], len(instrs))
                for i in instrs:
                    i.then_inc(sems[st["key"]], 16)
            else:
                assert len(instrs) >= 1, st["tag"]
                instrs[-1].then_inc(sems[st["key"]], 1)


class Slab:
    def __init__(self, total):
        self.free_l = [(0, total)]
        self.total = total
        self.peak = 0
        self.used = 0

    def alloc(self, n, tag="", align=1):
        for i, (off, sz) in enumerate(self.free_l):
            pad = (-off) % align
            if sz >= n + pad:
                if pad:
                    self.free_l[i] = (off, pad)
                    self.free_l.insert(i + 1, (off + pad + n, sz - pad - n))
                    if self.free_l[i + 1][1] == 0:
                        self.free_l.pop(i + 1)
                elif sz == n:
                    self.free_l.pop(i)
                else:
                    self.free_l[i] = (off + n, sz - n)
                self.used += n
                self.peak = max(self.peak, self.used + pad)
                return off + pad
        raise MemoryError(f"slab OOM {n} ({tag}); used={self.used} free={self.free_l}")

    def release(self, off, n):
        self.used -= n
        self.free_l.append((off, n))
        self.free_l.sort()
        m = []
        for o, s in self.free_l:
            if m and m[-1][0] + m[-1][1] == o:
                m[-1] = (m[-1][0], m[-1][1] + s)
            else:
                m.append((o, s))
        self.free_l = m


class Tile:
    def __init__(self, slab, sb, n, tag="", off=None, alloc_n=None, align=1):
        self.slab, self.sb, self.n, self.tag = slab, sb, n, tag
        self.alloc_n = alloc_n if alloc_n is not None else n
        self.off = slab.alloc(self.alloc_n, tag, align) if off is None else off

    def ap(self):
        return self.sb[:, self.off:self.off + self.n]

    def c(self, a, b):
        return self.sb[:, self.off + a:self.off + b]

    def r(self, a, b):
        return self.sb[:, self.off + a:self.off + b].bitcast(F32R)

    def b3(self, w, per):
        nb = self.n // per
        return self.sb[:, self.off:self.off + nb * per].rearrange(
            "p (a b) -> p a b", b=per)[:, :, 0:w]


    def row(self, a, b):
        return self.sb[0:1, self.off + a:self.off + b]

    def free(self):
        if self.slab is not None:
            self.slab.release(self.off, self.alloc_n)


def build_nc(nseg=8, dbg=False, skip_aprep=False, skip_fprep=False, emit_limit=None):
    L = nseg * S
    nc = bass.Bass("TRN2", target_bir_lowering=False)

    xT = nc.dram_tensor("xT", [D, 1 + L], F32, kind="ExternalInput")
    Wmat = {nm: nc.dram_tensor(f"W{nm}", [D, D], F32, kind="ExternalInput")
            for nm in ("k", "v", "r", "o")}
    tmk = nc.dram_tensor("tmk", [128, NB], F32, kind="ExternalInput")
    tmv = nc.dram_tensor("tmv", [128, NB], F32, kind="ExternalInput")
    tmr = nc.dram_tensor("tmr", [128, NB], F32, kind="ExternalInput")
    tdec = nc.dram_tensor("tdec", [128, NB], F32, kind="ExternalInput")
    tfir = nc.dram_tensor("tfir", [128, NB], F32, kind="ExternalInput")
    wf = {nm: nc.dram_tensor(nm, [H, D] if nm in ("w1", "w2") else
                             ([D, H] if nm == "w3" else [H, H]), F32,
                             kind="ExternalInput")
          for nm in ("w1", "w2", "h1", "h2", "w3")}
    wsc = {nm: nc.dram_tensor(nm + "s", [128, HB], F32, kind="ExternalInput")
           for nm in ("w1", "w2", "h1", "h2")}
    w3s = nc.dram_tensor("w3s", [128, NB], F32, kind="ExternalInput")
    selp = nc.dram_tensor("selp", [128, 64], F32, kind="ExternalInput")
    sels = nc.dram_tensor("sels", [128, 64], F32, kind="ExternalInput")
    outT = nc.dram_tensor("outT", [D, L], F32, kind="ExternalOutput")

    wfb = {nm: nc.dram_tensor(nm + "b", list(wf[nm].shape), BF16)
           for nm in wf}
    wfT = {"w1": nc.dram_tensor("w1T", [HB, NB, 128, 128], F32R),
           "w2": nc.dram_tensor("w2T", [HB, NB, 128, 128], F32R),
           "h1": nc.dram_tensor("h1T", [HB, HB, 128, 128], F32R),
           "h2": nc.dram_tensor("h2T", [HB, HB, 128, 128], F32R),
           "w3": nc.dram_tensor("w3T", [NB, HB, 128, 128], F32R)}
    att_h = {nm: nc.dram_tensor(f"a{nm}h", [NB, NB, 128, 128], F32R)
             for nm in ("k", "v", "r", "o")}
    att_l = {nm: nc.dram_tensor(f"a{nm}l", [NB, NB, 128, 128], F32R)
             for nm in ("k", "v", "r", "o")}
    spkind = "ExternalOutput" if dbg else "Internal"
    sp = {nm: nc.dram_tensor(f"sp_{nm}", [nseg, 128, NB * S], F32, kind=spkind)
          for nm in ("ek", "v", "r", "al", "bv", "F", "base", "x2")}
    dbg_x2 = (nc.dram_tensor("dbg_x2", [nseg, 128, NB * S], F32,
                             kind="ExternalOutput") if dbg else None)
    dbg_fx = (nc.dram_tensor("dbg_fx", [nseg, 128, NB * S], F32,
                             kind="ExternalOutput") if dbg else None)
    dbg_E = (nc.dram_tensor("dbg_E", [nseg, 128, NB * (S + 1)], F32,
                            kind="ExternalOutput") if dbg else None)
    dbg_xl = (nc.dram_tensor("dbg_xl", [128, NB], F32,
                             kind="ExternalOutput") if dbg else None)
    dbg_ew = (nc.dram_tensor("dbg_ew", [128, NB * (S // 2)], F32,
                             kind="ExternalOutput") if dbg else None)
    dbg_ew2 = (nc.dram_tensor("dbg_ew2", [128, NB * (S // 2)], F32,
                              kind="ExternalOutput") if dbg else None)
    dbg_ew3 = (nc.dram_tensor("dbg_ew3", [128, NB * (S // 2)], F32,
                              kind="ExternalOutput") if dbg else None)
    dbg_pa = (nc.dram_tensor("dbg_pa", [2, 128, HB * FS], F32,
                             kind="ExternalOutput") if dbg else None)
    dbg_gs = (nc.dram_tensor("dbg_gs", [2, 128, FS], F32,
                             kind="ExternalOutput") if dbg else None)
    dbg_wl = (nc.dram_tensor("dbg_wl", [128, 2 * 1024], F32,
                             kind="ExternalOutput") if dbg else None)
    dbg_wt = (nc.dram_tensor("dbg_wt", [NB, 128, 128], F32,
                             kind="ExternalOutput") if dbg else None)
    dbg_fbf = (nc.dram_tensor("dbg_fbf", [128, H], BF16,
                              kind="ExternalOutput") if dbg else None)
    dbg_qf = (nc.dram_tensor("dbg_qf", [6, 128, 64], F32,
                             kind="ExternalOutput") if dbg else None)
    dbg_wb = (nc.dram_tensor("dbg_wb", [256, D], BF16,
                             kind="ExternalOutput") if dbg else None)
    dbg_fxp = (nc.dram_tensor("dbg_fxp", [2, 128, NB * S], F32,
                              kind="ExternalOutput") if dbg else None)
    dbg_pb = (nc.dram_tensor("dbg_pb", [2, 128, HB * FS], F32,
                             kind="ExternalOutput") if dbg else None)
    ag1i = nc.dram_tensor("ag1i", [1, D], F32)
    ag1o = nc.dram_tensor("ag1o", [8, D], F32)
    ag2i = nc.dram_tensor("ag2i", [1, 2 * D], F32)
    ag2o = nc.dram_tensor("ag2o", [8, 2 * D], F32)
    eye_d = nc.inline_tensor(np.eye(128, dtype=np.float32), "eye")

    SCR = nc.alloc_sbuf_tensor("scanscr", [128, NB * (S // 2)], F32, align_bytes=2048)
    TOT = 50600
    SB = nc.alloc_sbuf_tensor("slab", [128, TOT], F32)
    slab = Slab(TOT)

    allocs_log = []

    def T_(n, tag=""):
        alloc_n = NB * (S + 1) if n == NB * S else n
        t = Tile(slab, SB, n, tag, alloc_n=alloc_n)
        allocs_log.append((t.off, t.alloc_n, tag))
        return t

    ones = T_(1, "ones")
    onesr = T_(128, "onesr")
    C = {nm: T_(NB, nm) for nm in
         ("tmk", "tmv", "tmr", "w", "u", "ew", "eu", "w3s",
          "Elast", "qlast", "xlast", "Acore", "Aseg", "Ein", "qin",
          "h1", "h2")}
    SC = {nm: T_(HB, nm) for nm in ("w1", "w2", "h1", "h2")}
    T64 = {nm: T_(64, nm) for nm in ("selp", "sels", "Mtab", "Atab", "qtab",
                                     "EA", "Qt", "tmp")}
    R_ = {nm: T_(S, "R" + nm) for nm in ("mean", "var", "rstd")}
    R_["m2"] = R_["rstd"]
    eye_t = T_(128, "eye")

    pb = [nc.alloc_psum_tensor(f"pb{i}", [128, S], F32) for i in range(8)]

    sch = Sched()

    def mm(e, out, lhsT, rhs, start, stop):
        return e.matmul(out, lhsT, rhs, start=start, stop=stop)

    def scan8(e, out_fn, d0_fn, d1_fn, init_fn, op0, op1):
        h = S // 2
        out = []
        out.append(e.tensor_tensor_scan(
            SCR[:, 0:h], d0_fn(0), d1_fn(0)[:, 0:h], init_fn(0), op0, op1))
        for b in range(NB):
            out.append(e.tensor_tensor_scan(
                SCR[:, b * h:b * h + h], d0_fn(b), d1_fn(b)[:, 0:h],
                init_fn(b), op0, op1))
        for b in range(NB):
            out.append(e.tensor_tensor_scan(
                out_fn(b)[:, h:S], d0_fn(b), d1_fn(b)[:, h:S],
                SCR[:, b * h + h - 1:b * h + h], op0, op1))
        for b in range(NB):
            out.append(e.tensor_copy(out_fn(b)[:, 0:h], SCR[:, b * h:b * h + h]))
        return out

    def scan8w(e, out_fn, d0_fn, d1_fn, init_fn, op0, op1):
        h = S // 2
        out = []
        out.append(e.tensor_tensor_scan(
            SCR[:, 0:h], d0_fn(0)[:, 0:h], d1_fn(0)[:, 0:h], init_fn(0),
            op0, op1))
        for b in range(NB):
            out.append(e.tensor_tensor_scan(
                SCR[:, b * h:b * h + h], d0_fn(b)[:, 0:h], d1_fn(b)[:, 0:h],
                init_fn(b), op0, op1))
        for b in range(NB):
            out.append(e.tensor_tensor_scan(
                out_fn(b)[:, h:S], d0_fn(b)[:, h:S], d1_fn(b)[:, h:S],
                SCR[:, b * h + h - 1:b * h + h], op0, op1))
        for b in range(NB):
            out.append(e.tensor_copy(out_fn(b)[:, 0:h], SCR[:, b * h:b * h + h]))
        return out

    _ld_parts = [
        lambda e: e.dma_start(out=C["tmk"].ap(), in_=tmk.ap()),
        lambda e: e.dma_start(out=C["tmv"].ap(), in_=tmv.ap()),
        lambda e: e.dma_start(out=C["tmr"].ap(), in_=tmr.ap()),
        lambda e: e.dma_start(out=C["w"].ap(), in_=tdec.ap()),
        lambda e: e.dma_start(out=C["u"].ap(), in_=tfir.ap()),
        lambda e: e.dma_start(out=SC["w1"].ap(), in_=wsc["w1"].ap()),
        lambda e: e.dma_start(out=SC["w2"].ap(), in_=wsc["w2"].ap()),
        lambda e: e.dma_start(out=SC["h1"].ap(), in_=wsc["h1"].ap()),
        lambda e: e.dma_start(out=SC["h2"].ap(), in_=wsc["h2"].ap()),
        lambda e: e.dma_start(out=C["w3s"].ap(), in_=w3s.ap()),
        lambda e: e.dma_start(out=T64["selp"].ap(), in_=selp.ap()),
        lambda e: e.dma_start(out=T64["sels"].ap(), in_=sels.ap()),
        lambda e: e.dma_start(out=eye_t.ap(), in_=eye_d.ap()),
    ]
    ld_misc = None
    for _i, _f in enumerate(_ld_parts):
        ld_misc = sch.dma("sync", _f, deps=[ld_misc], n=1, tag=f"ldm{_i}")
    _unused = lambda e: [
        e.dma_start(out=C["tmk"].ap(), in_=tmk.ap()),
        e.dma_start(out=C["tmv"].ap(), in_=tmv.ap()),
        e.dma_start(out=C["tmr"].ap(), in_=tmr.ap()),
        e.dma_start(out=C["w"].ap(), in_=tdec.ap()),
        e.dma_start(out=C["u"].ap(), in_=tfir.ap()),
        e.dma_start(out=SC["w1"].ap(), in_=wsc["w1"].ap()),
        e.dma_start(out=SC["w2"].ap(), in_=wsc["w2"].ap()),
        e.dma_start(out=SC["h1"].ap(), in_=wsc["h1"].ap()),
        e.dma_start(out=SC["h2"].ap(), in_=wsc["h2"].ap()),
        e.dma_start(out=C["w3s"].ap(), in_=w3s.ap()),
        e.dma_start(out=T64["selp"].ap(), in_=selp.ap()),
        e.dma_start(out=T64["sels"].ap(), in_=sels.ap()),
        e.dma_start(out=eye_t.ap(), in_=eye_d.ap()),
    ]

    def dek(e, hi, lo, x):
        return [
            e.tensor_scalar_mul(hi, x, 2049.0),
            e.tensor_tensor(lo, hi, x, ALU.subtract),
            e.tensor_tensor(hi, hi, lo, ALU.subtract),
            e.tensor_tensor(lo, x, hi, ALU.subtract),
        ]

    def fill(e, out_ap, const, src):
        return e.tensor_scalar(out_ap, src, 0.0, const, ALU.mult, ALU.add)

    init_c = sch.comp("vector", lambda e: [
        fill(e, ones.ap(), 1.0, C["tmk"].c(0, 1)),
        fill(e, onesr.sb[0:1, onesr.off:onesr.off + 128], 1.0,
             eye_t.sb[0:1, eye_t.off:eye_t.off + 128]),
        fill(e, C["Acore"].ap(), 0.0, C["tmk"].ap()),
    ], deps=[ld_misc], tag="init_c")

    exp_wu = sch.comp("scalar", lambda e: [
        e.activation(C["ew"].ap(), C["w"].ap(), AF.Exp),
        e.activation(C["eu"].ap(), C["u"].ap(), AF.Exp),
    ], deps=[ld_misc], tag="exp_wu")

    EW = S // 2
    canary = T_(512, "canary")
    can_c = sch.comp("vector", lambda e: e.memset(canary.ap(), 7.0),
                     tag="canary_set")
    ewt = T_(NB * EW, "ewt")
    ewt_c = sch.comp("vector", lambda e: [
        fill(e, SCR[:, 0:EW], 1.0, C["tmk"].c(0, 1).broadcast_to([128, EW])),
    ] + [
        e.tensor_scalar_mul(ewt.c(b * EW, (b + 1) * EW),
                            SCR[:, 0:EW],
                            C["ew"].c(b, b + 1))
        for b in range(NB)
    ], deps=[init_c, exp_wu], tag="ewt")

    pstage = T_(D, "pstage")
    phi = T_(D, "phi")
    plo = T_(D, "plo")
    attn_prep_done = {}
    prev_fin = ld_misc
    for nm in ([] if skip_aprep else ["k", "v", "r", "o"]):
        Wsrc = Wmat[nm]
        for db in range(NB):
            ld = sch.dma("sync", (lambda Wsrc=Wsrc, db=db: lambda e: e.dma_start(
                out=pstage.ap().rearrange("p (e d) -> p e d", d=128),
                in_=Wsrc.ap().rearrange("(eb p) d -> p eb d", p=128)
                    [:, :, db * 128:(db + 1) * 128]))(),
                deps=[prev_fin], n=1, tag=f"ap_ld_{nm}{db}")
            tp = sch.comp("tensor", (lambda: lambda e: [
                e.transpose(pb[eb % 4][:, (eb // 4) * 128:(eb // 4) * 128 + 128],
                            pstage.c(eb * 128, (eb + 1) * 128), eye_t.ap())
                for eb in range(NB)
            ])(), deps=[ld, ld_misc], tag=f"ap_tp_{nm}{db}")
            ev = sch.comp("scalar", (lambda: lambda e: [
                e.activation(pstage.c(eb * 128, (eb + 1) * 128),
                             pb[eb % 4][:, (eb // 4) * 128:(eb // 4) * 128 + 128],
                             AF.Copy)
                for eb in range(NB)
            ])(), deps=[tp], tag=f"ap_ev_{nm}{db}")
            spl = sch.comp("vector", (lambda: lambda e: dek(
                e, phi.ap(), plo.ap(), pstage.ap()))(),
                deps=[ev], tag=f"ap_spl_{nm}{db}")
            fin = sch.dma("sync", (lambda nm=nm, db=db: lambda e: [
                e.dma_start(out=att_h[nm].ap()[db].rearrange("e p c -> p e c"),
                            in_=phi.r(0, D).rearrange("p (e c) -> p e c", c=128)),
                e.dma_start(out=att_l[nm].ap()[db].rearrange("e p c -> p e c"),
                            in_=plo.r(0, D).rearrange("p (e c) -> p e c", c=128)),
            ])(), deps=[spl], n=2, tag=f"ap_st_{nm}{db}")
            prev_fin = fin
        attn_prep_done[nm] = prev_fin
    if skip_aprep:
        attn_prep_done = {nm: None for nm in ("k", "v", "r", "o")}

    cast_steps = {nm: (None if skip_fprep else
                       sch.dma("gpsimd", (lambda nm=nm: lambda e: e.dma_start(
                           out=wfb[nm].ap(), in_=wf[nm].ap()))(), n=1,
                           tag=f"fc_{nm}"))
                  for nm in wf}

    fbf = Tile(slab, SB, H // 2, "fbf", align=8)
    allocs_log.append((fbf.off, fbf.alloc_n, "fbf"))
    frr = T_(H, "frr")
    ffn_prep_done = {}
    prev_cv = None
    prev_st = None
    for nm in ([] if skip_fprep else ["w1", "w2", "h1", "h2", "w3"]):
        rows = wf[nm].shape[0]
        cols = wf[nm].shape[1]
        ndb = cols // 128
        for db in range(ndb):
            tpd = sch.dma("scalar", (lambda nm=nm, db=db, rows=rows: lambda e:
                e.dma_start(out=fbf.ap().bitcast(BF16)[:, 0:rows],
                            in_=wfb[nm].ap()[:, db * 128:(db + 1) * 128],
                            transpose=True))(),
                deps=[cast_steps[nm], prev_cv], n=1, tag=f"fp_tp_{nm}{db}")
            if dbg and nm == "w1" and db == 0:
                sch.dma("sync", lambda e: e.dma_start(
                    out=dbg_fbf.ap(), in_=fbf.ap().bitcast(BF16)[:, 0:H]),
                    deps=[tpd], n=1, tag="dbg_fbf")
            cv = sch.comp("vector", (lambda rows=rows: lambda e:
                e.tensor_copy(frr.r(0, rows),
                              fbf.ap().bitcast(BF16)[:, 0:rows]))(),
                deps=[tpd, prev_st], tag=f"fp_cv_{nm}{db}")
            stp = sch.dma("sync", (lambda nm=nm, db=db, rows=rows: lambda e:
                e.dma_start(out=wfT[nm].ap()[:, db].rearrange("o p c -> p o c"),
                            in_=frr.r(0, rows).rearrange("p (o c) -> p o c", c=128)))(),
                deps=[cv], n=1, tag=f"fp_st_{nm}{db}")
            prev_cv, prev_st = cv, stp
        ffn_prep_done[nm] = prev_st
    if skip_fprep:
        ffn_prep_done = {nm: None for nm in ("w1", "w2", "h1", "h2", "w3")}

    if dbg:
        sch.dma("sync", lambda e: e.dma_start(out=dbg_ew.ap(),
                                              in_=ewt.c(0, NB * EW)),
                deps=[ewt_c, can_c], n=1, tag="dbg_ew")

    if dbg and not skip_fprep:
        sch.dma("gpsimd", lambda e: [
            e.dma_start(out=dbg_wt.ap(),
                        in_=wfT["w1"].ap()[0].bitcast(F32)),
            e.dma_start(out=dbg_wb.ap(), in_=wfb["w1"].ap()[0:256, :]),
        ], deps=[ffn_prep_done["w1"]], n=2, tag="dbg_wt")

    prep_fence = sch.comp("gpsimd", lambda e: e.engine_nop(),
                          deps=[attn_prep_done["o"], ffn_prep_done["w3"]],
                          tag="prep_fence")
    for t in (pstage, phi, plo, fbf, frr, eye_t):
        t.free()

    xw = T_(NB * S, "xw")
    xsq = T_(NB * S, "xsq")
    xln = T_(NB * S, "xln")
    mixt = T_(NB * S, "mix")
    mixhi = T_(NB * S, "mixhi")
    mixlo = T_(NB * S, "mixlo")
    ek_t = T_(NB * S, "ek")
    v_t = T_(NB * S, "v")
    r_t = T_(NB * S, "r")
    E_t = T_(NB * (S + 1), "E")
    wsl = [T_(2 * D, f"wsl{p}") for p in range(2)]

    def x_seg(s):
        return xT.ap().rearrange("(b p) t -> p b t", p=128)[
            :, :, 1 + s * S:1 + (s + 1) * S]

    def strided_last(tile, per):
        return tile.ap().rearrange("p (a b) -> p a b", b=per)[:, :, per - 1:per]

    def bound_col(tile):
        return tile.ap().rearrange("p (a b) -> p a b", b=S + 1)[:, :, 0:1]

    def col8(tile):
        return tile.ap().rearrange("p (a b) -> p a b", b=1)

    def ln_emit(tagpre, s, src, xsq_t, dst, dep_src, dep_banks, dep_sq_extra,
                dep_norm_extra, extra_norm_ops=None):
        sq = sch.comp("scalar", (lambda: lambda e: e.activation(
            xsq_t.ap(), src.ap(), AF.Square))(),
            deps=[dep_src, dep_sq_extra], tag=f"{tagpre}_sq{s}")
        stats = sch.comp("tensor", (lambda: lambda e: [
            mm(e, pb[0][0:1, :], ones.c(0, 1), src.c(b * S, (b + 1) * S),
               b == 0, b == NB - 1) for b in range(NB)
        ] + [
            mm(e, pb[1][0:1, :], ones.c(0, 1), xsq_t.c(b * S, (b + 1) * S),
               b == 0, b == NB - 1) for b in range(NB)
        ])(), deps=[dep_src, sq, init_c, dep_banks], tag=f"{tagpre}_st{s}")
        rstat = sch.comp("vector", (lambda: lambda e: [
            e.tensor_scalar_mul(R_["mean"].row(0, S), pb[0][0:1, :], I1024),
            e.tensor_mul(R_["m2"].row(0, S), R_["mean"].row(0, S),
                         R_["mean"].row(0, S)),
            e.tensor_scalar_mul(R_["var"].row(0, S), pb[1][0:1, :], I1024),
            e.tensor_tensor(R_["var"].row(0, S), R_["var"].row(0, S),
                            R_["m2"].row(0, S), ALU.subtract),
            e.tensor_scalar_add(R_["var"].row(0, S), R_["var"].row(0, S),
                                EPS_LN),
        ])(), deps=[stats], tag=f"{tagpre}_rs{s}")
        rstd = sch.comp("scalar", (lambda: lambda e: [
            e.activation(R_["rstd"].row(0, S), R_["var"].row(0, S), AF.Ln),
            e.activation(R_["rstd"].row(0, S), R_["rstd"].row(0, S),
                         AF.Exp, scale=-0.5),
        ])(), deps=[rstat], tag=f"{tagpre}_rd{s}")
        bc = sch.comp("tensor", (lambda: lambda e: [
            mm(e, pb[2][:, :], onesr.sb[0:1, onesr.off:onesr.off + 128],
               R_["mean"].row(0, S), True, True),
            mm(e, pb[3][:, :], onesr.sb[0:1, onesr.off:onesr.off + 128],
               R_["rstd"].row(0, S), True, True),
        ])(), deps=[rstd, rstat], tag=f"{tagpre}_bc{s}")
        def emit_norm(e):
            out = []
            for b in range(NB):
                out.append(e.tensor_tensor(dst.c(b * S, (b + 1) * S),
                                           src.c(b * S, (b + 1) * S),
                                           pb[2][:, :], ALU.subtract))
            for b in range(NB):
                out.append(e.tensor_tensor(dst.c(b * S, (b + 1) * S),
                                           dst.c(b * S, (b + 1) * S),
                                           pb[3][:, :], ALU.mult))
            if extra_norm_ops:
                out.extend(extra_norm_ops(e))
            return out
        norm = sch.comp("vector", emit_norm, deps=[bc, dep_norm_extra],
                        tag=f"{tagpre}_nm{s}")
        return norm

    def attn_stream(tag, nm, rhs_hi, rhs_lo, dep_rhs, dep_banks, dep_slots, slots=None):
        slots = slots or wsl
        ld_ids = {}
        mm_ids = {}
        for db in range(NB):
            par = db % 2
            deps_ld = [attn_prep_done[nm]]
            if db < 2:
                deps_ld.append(dep_slots)
            else:
                deps_ld.append(mm_ids[db - 2])
            ld = sch.dma("sync", (lambda nm=nm, db=db, par=par: lambda e: [
                e.dma_start(out=slots[par].r(0, D).rearrange("p (e c) -> p e c", c=128),
                            in_=att_h[nm].ap()[db].rearrange("e p c -> p e c")),
                e.dma_start(out=slots[par].r(D, 2 * D).rearrange("p (e c) -> p e c", c=128),
                            in_=att_l[nm].ap()[db].rearrange("e p c -> p e c")),
            ])(), deps=deps_ld, n=2, tag=f"{tag}_ld{db}")
            ld_ids[db] = ld
            deps_mm = [ld, dep_rhs]
            if db == 0:
                deps_mm.append(dep_banks)
            def emit_mm(e, db=db, par=par):
                out = []
                for eb in range(NB):
                    whi = slots[par].r(eb * 128, (eb + 1) * 128)
                    wlo = slots[par].r(D + eb * 128, D + (eb + 1) * 128)
                    xh = rhs_hi(db)
                    xl = rhs_lo(db)
                    out.append(mm(e, pb[eb][:, :], whi, xh, db == 0, False))
                    out.append(mm(e, pb[eb][:, :], whi, xl, False, False))
                    out.append(mm(e, pb[eb][:, :], wlo, xh, False, db == NB - 1))
                return out
            mm_ids[db] = sch.comp("tensor", emit_mm, deps=deps_mm,
                                  tag=f"{tag}_mm{db}")
        return mm_ids[NB - 1]

    prev_norm = None
    prev_lastmix = prep_fence
    prev_evacs = None
    prev_esc = None
    prev_spill = None
    halo_done = None
    for s in range(nseg):
        ldx = sch.dma("sync", (lambda s=s: lambda e: e.dma_start(
            out=xw.c(0, NB * S).rearrange("p (b t) -> p b t", t=S),
            in_=x_seg(s)))(), deps=[prev_norm, prep_fence], n=1,
            tag=f"p1_ldx{s}")
        if s == 0:
            hst1 = C["h1"]
            hst2 = C["h2"]
            ldh = sch.dma("sync", lambda e: e.dma_start(
                out=hst1.ap().rearrange("p (b t) -> p b t", t=1),
                in_=xT.ap().rearrange("(b p) t -> p b t", p=128)[:, :, 0:1],
                allow_slow_non_contiguous=True),
                deps=[prep_fence], n=1, tag="p1_ldh")
            hsq = sch.comp("scalar", lambda e: e.activation(
                hst2.ap(), hst1.ap(), AF.Square), deps=[ldh], tag="h_sq")
            hstt = sch.comp("tensor", lambda e: [
                mm(e, pb[0][0:1, 0:NB], ones.c(0, 1), hst1.ap(), True, True),
                mm(e, pb[1][0:1, 0:NB], ones.c(0, 1), hst2.ap(), True, True),
            ], deps=[ldh, hsq, init_c, attn_prep_done["o"]], tag="h_st")
            hred = sch.comp("vector", lambda e: [
                e.reduce_sum(R_["mean"].row(0, 1), pb[0][0:1, 0:NB], AX.X),
                e.reduce_sum(R_["var"].row(0, 1), pb[1][0:1, 0:NB], AX.X),
                e.tensor_scalar_mul(R_["mean"].row(0, 1), R_["mean"].row(0, 1),
                                    I1024),
                e.tensor_scalar_mul(R_["var"].row(0, 1), R_["var"].row(0, 1),
                                    I1024),
                e.tensor_mul(R_["m2"].row(0, 1), R_["mean"].row(0, 1),
                             R_["mean"].row(0, 1)),
                e.tensor_tensor(R_["var"].row(0, 1), R_["var"].row(0, 1),
                                R_["m2"].row(0, 1), ALU.subtract),
                e.tensor_scalar_add(R_["var"].row(0, 1), R_["var"].row(0, 1),
                                    EPS_LN),
            ], deps=[hstt], tag="h_red")
            hrs = sch.comp("scalar", lambda e: [
                e.activation(R_["rstd"].row(0, 1), R_["var"].row(0, 1), AF.Ln),
                e.activation(R_["rstd"].row(0, 1), R_["rstd"].row(0, 1),
                             AF.Exp, scale=-0.5),
            ], deps=[hred], tag="h_rd")
            hbc = sch.comp("tensor", lambda e: [
                mm(e, pb[2][:, 0:1], onesr.sb[0:1, onesr.off:onesr.off + 128],
                   R_["mean"].row(0, 1), True, True),
                mm(e, pb[3][:, 0:1], onesr.sb[0:1, onesr.off:onesr.off + 128],
                   R_["rstd"].row(0, 1), True, True),
            ], deps=[hrs, hred], tag="h_bc")
            hnorm = sch.comp("vector", lambda e: [
                e.tensor_copy(C["Aseg"].c(0, 1), pb[2][:, 0:1]),
                e.tensor_copy(C["Aseg"].c(1, 2), pb[3][:, 0:1]),
                e.tensor_scalar_sub(C["xlast"].ap(), hst1.ap(),
                                    C["Aseg"].c(0, 1)),
                e.tensor_scalar_mul(C["xlast"].ap(), C["xlast"].ap(),
                                    C["Aseg"].c(1, 2)),
            ], deps=[hbc], tag="h_nm")
            if dbg:
                sch.dma("sync", lambda e: e.dma_start(
                    out=dbg_xl.ap(), in_=C["xlast"].ap()),
                    deps=[hnorm], n=1, tag="dbg_xl")
            halo_done = hnorm
        dep_banks = halo_done if s == 0 else prev_evacs
        norm = ln_emit("ln1", s, xw, xsq, xln, ldx, dep_banks,
                       prev_lastmix, prev_lastmix)
        prev_norm = norm

        evacs = {}
        lastmm = None
        for nm, tmc in (("k", "tmk"), ("v", "tmv"), ("r", "tmr")):
            mix_deps = [norm]
            if lastmm is not None:
                mix_deps.append(lastmm)
            def emit_mix(e, tmc=tmc):
                out = []
                out.append(e.tensor_sub(xsq.b3(S, S)[:, :, 1:S],
                                        xln.b3(S, S)[:, :, 1:S],
                                        xln.b3(S, S)[:, :, 0:S - 1]))
                out.append(e.tensor_sub(xsq.b3(1, S), xln.b3(1, S),
                                        col8(C["xlast"])))
                for b in range(NB):
                    out.append(e.tensor_scalar_mul(
                        mixt.c(b * S, (b + 1) * S), xsq.c(b * S, (b + 1) * S),
                        C[tmc].c(b, b + 1)))
                out.append(e.tensor_add(mixt.b3(S, S)[:, :, 1:S],
                                        mixt.b3(S, S)[:, :, 1:S],
                                        xln.b3(S, S)[:, :, 0:S - 1]))
                out.append(e.tensor_add(mixt.b3(1, S), mixt.b3(1, S),
                                        col8(C["xlast"])))
                out.extend(dek(e, mixhi.c(0, NB * S), mixlo.c(0, NB * S),
                               mixt.c(0, NB * S)))
                return out
            mixs = sch.comp("vector", emit_mix, deps=mix_deps, tag=f"mx_{nm}{s}")
            dep_bk = norm if nm == "k" else evacs[prev_nm]
            dep_slots = (prev_evacs or prep_fence) if nm == "k" else lastmm
            lastmm = attn_stream(f"pj{nm}{s}", nm,
                                 lambda db: mixhi.r(db * S, (db + 1) * S),
                                 lambda db: mixlo.r(db * S, (db + 1) * S),
                                 mixs, dep_bk, dep_slots)
            if nm == "k":
                evd = [lastmm, prev_esc, prev_spill]
                emit_ev = lambda e: [e.activation(ek_t.c(b * S, (b + 1) * S),
                                                  pb[b][:, :], AF.Exp)
                                     for b in range(NB)]
            elif nm == "v":
                evd = [lastmm, prev_spill]
                emit_ev = lambda e: [e.activation(v_t.c(b * S, (b + 1) * S),
                                                  pb[b][:, :], AF.Copy)
                                     for b in range(NB)]
            else:
                evd = [lastmm, prev_spill]
                emit_ev = lambda e: [e.activation(r_t.c(b * S, (b + 1) * S),
                                                  pb[b][:, :], AF.Sigmoid)
                                     for b in range(NB)]
            evacs[nm] = sch.comp("scalar", emit_ev, deps=evd, tag=f"ev{nm}{s}")
            prev_nm = nm
        prev_evacs = evacs["r"]
        prev_lastmix = None
        prev_lastmix = mixs

        xlu = sch.comp("vector", (lambda: lambda e: e.tensor_copy(
            col8(C["xlast"]), xln.b3(S, S)[:, :, S - 1:S]))(),
            deps=[norm], tag=f"xl{s}")

        def emit_escan(e, s=s):
            out = []
            out.extend(scan8(
                e,
                lambda b: E_t.c(b * (S + 1) + 1, (b + 1) * (S + 1)),
                lambda b: ewt.c(b * EW, (b + 1) * EW),
                lambda b: ek_t.c(b * S, (b + 1) * S),
                (lambda b: 0.0) if s == 0 else (lambda b: C["Elast"].c(b, b + 1)),
                ALU.mult, ALU.max))
            out.append(e.tensor_copy(col8(C["Elast"]),
                                     strided_last(E_t, S + 1)))
            return out
        if dbg and s == 0:
            sch.dma("sync", lambda e: e.dma_start(
                out=dbg_ew2.ap(), in_=ewt.c(0, NB * (S // 2))),
                deps=[evacs["k"]], n=1, tag="dbg_ewA")
        esc = sch.comp("vector", emit_escan,
                       deps=[evacs["k"], prev_esc, ewt_c]
                            + list(cast_steps.values()), tag=f"esc{s}")
        prev_esc = esc

        if dbg and s == 0:
            sch.dma("sync", lambda e: e.dma_start(
                out=dbg_ew3.ap(), in_=ewt.c(0, NB * (S // 2))),
                deps=[esc], n=1, tag="dbg_ewB")
        spill = sch.dma("sync", (lambda s=s: lambda e: [
            e.dma_start(out=sp["ek"].ap()[s], in_=ek_t.c(0, NB * S)),
            e.dma_start(out=sp["v"].ap()[s], in_=v_t.c(0, NB * S)),
            e.dma_start(out=sp["r"].ap()[s], in_=r_t.c(0, NB * S)),
        ])(), deps=[evacs["k"], evacs["v"], evacs["r"], esc], n=3,
            tag=f"sp1_{s}")
        prev_spill = spill

    ph1_done_v = prev_esc
    ph1_done = prev_spill

    ag1_ld = sch.dma("gpsimd", lambda e: e.dma_start(
        out=ag1i.ap().rearrange("a (b p) -> p a b", p=128),
        in_=C["Elast"].ap().rearrange("p (a b) -> p a b", a=1),
                    allow_slow_non_contiguous=True),
        deps=[ph1_done_v], n=1, tag="ag1_ld")
    ag1 = sch.add("gpsimd", "cc", lambda e: e.collective_compute(
        "AllGather", ALU.bypass, replica_groups=[list(range(8))],
        ins=[ag1i.ap()], outs=[ag1o.ap()]), deps=[ag1_ld], tag="ag1")
    ag1_st = sch.dma("gpsimd", lambda e: [
        e.dma_start(
            out=T64["Mtab"].ap().rearrange("p (b j) -> p b j", j=8)[:, :, j:j + 1],
            in_=ag1o.ap().rearrange("j (b p) -> p b j", p=128)[:, :, j:j + 1],
            allow_slow_non_contiguous=True)
        for j in range(8)], deps=[ag1], n=8, tag="ag1_st")
    def emit_ein(e):
        out = [e.tensor_mul(T64["tmp"].ap(), T64["Mtab"].ap(), T64["selp"].ap())]
        t3 = T64["tmp"].ap().rearrange("p (b j) -> p b j", j=8)
        e3 = C["Ein"].ap().rearrange("p (b j) -> p b j", j=1)
        out.append(e.tensor_copy(e3, t3[:, :, 0:1]))
        for j in range(1, 8):
            out.append(e.tensor_add(e3, e3, t3[:, :, j:j + 1]))
        return out
    ein = sch.comp("gpsimd", emit_ein, deps=[ag1_st, ld_misc], tag="ein")

    for t in (xw, xsq, xln, mixt, mixhi, mixlo, wsl[0], wsl[1]):
        t.free()

    rE = T_(NB * (S + 1), "rE")
    eku = T_(NB * S, "eku")
    EPt = T_(NB * S, "EP")
    so_t = T_(NB * S, "so")
    al_t = T_(NB * S, "al")
    bv_t = T_(NB * S, "bv")
    q_t = T_(NB * (S + 1), "q")

    def E_prev(b):
        return E_t.c(b * (S + 1), b * (S + 1) + S)

    def E_cur(b):
        return E_t.c(b * (S + 1) + 1, (b + 1) * (S + 1))

    prev_v2 = ein
    prev_accA = None
    prev_sp2 = ph1_done
    for s in range(nseg):
        rld = sch.dma("sync", (lambda s=s: lambda e: [
            e.dma_start(out=ek_t.c(0, NB * S), in_=sp["ek"].ap()[s]),
            e.dma_start(out=v_t.c(0, NB * S), in_=sp["v"].ap()[s]),
        ])(), deps=[prev_sp2, prev_v2], n=2, tag=f"p2_rld{s}")

        def emit_esc2(e, s=s):
            out = []
            src = C["Ein"] if s == 0 else C["Elast"]
            out.append(e.tensor_copy(bound_col(E_t), col8(src)))
            out.extend(scan8(
                e, E_cur,
                lambda b: ewt.c(b * EW, (b + 1) * EW),
                lambda b: ek_t.c(b * S, (b + 1) * S),
                lambda b: src.c(b, b + 1), ALU.mult, ALU.max))
            out.append(e.tensor_copy(col8(C["Elast"]),
                                     strided_last(E_t, S + 1)))
            return out
        esc2 = sch.comp("vector", emit_esc2,
                        deps=[rld, ein],
                        tag=f"esc2_{s}")

        rE_s = sch.comp("scalar", (lambda: lambda e: [
            e.activation(rE.ap(), E_t.ap(), AF.Ln),
            e.activation(rE.ap(), rE.ap(), AF.Exp, scale=-1.0),
        ])(), deps=[esc2], tag=f"rE{s}")

        p2a = sch.comp("vector", (lambda: lambda e: [
            e.tensor_scalar_mul(eku.c(b * S, (b + 1) * S),
                                ek_t.c(b * S, (b + 1) * S),
                                C["eu"].c(b, b + 1)) for b in range(NB)
        ] + [
            e.tensor_max(EPt.c(b * S, (b + 1) * S),
                         eku.c(b * S, (b + 1) * S), E_prev(b))
            for b in range(NB)
        ])(), deps=[esc2, exp_wu], tag=f"p2a{s}")

        rEP = sch.comp("scalar", (lambda: lambda e: [
            e.activation(EPt.ap(), EPt.ap(), AF.Ln),
            e.activation(EPt.ap(), EPt.ap(), AF.Exp, scale=-1.0),
        ])(), deps=[p2a], tag=f"rEP{s}")

        p2b = sch.comp("vector", (lambda: lambda e: [
            e.tensor_mul(so_t.c(b * S, (b + 1) * S), E_prev(b),
                         EPt.c(b * S, (b + 1) * S)) for b in range(NB)
        ] + [
            e.tensor_scalar_max(so_t.ap(), so_t.ap(), CLAMP),
            e.tensor_mul(eku.ap(), eku.ap(), EPt.ap()),
            e.tensor_scalar_max(eku.ap(), eku.ap(), CLAMP),
            e.scalar_tensor_tensor(EPt.ap(), so_t.ap(), EPS_WKV, eku.ap(),
                                   ALU.add, ALU.add),
        ])(), deps=[rEP], tag=f"p2b{s}")

        rden = sch.comp("scalar", (lambda: lambda e: [
            e.activation(EPt.ap(), EPt.ap(), AF.Ln),
            e.activation(EPt.ap(), EPt.ap(), AF.Exp, scale=-1.0),
        ])(), deps=[p2b], tag=f"rdn{s}")

        p2c = sch.comp("vector", (lambda: lambda e: [
            e.tensor_mul(so_t.ap(), so_t.ap(), EPt.ap()),
            e.tensor_mul(eku.ap(), eku.ap(), v_t.ap()),
            e.tensor_mul(eku.ap(), eku.ap(), EPt.ap()),
        ] + [
            e.tensor_mul(al_t.c(b * S, (b + 1) * S), E_prev(b),
                         rE.c(b * (S + 1) + 1, (b + 1) * (S + 1)))
            for b in range(NB)
        ] + [
            e.tensor_scalar(al_t.c(b * S, (b + 1) * S),
                            al_t.c(b * S, (b + 1) * S),
                            C["ew"].c(b, b + 1), CLAMP, ALU.mult, ALU.max)
            for b in range(NB)
        ] + [
            e.tensor_mul(bv_t.c(b * S, (b + 1) * S),
                         ek_t.c(b * S, (b + 1) * S),
                         rE.c(b * (S + 1) + 1, (b + 1) * (S + 1)))
            for b in range(NB)
        ] + [
            e.tensor_scalar_max(bv_t.ap(), bv_t.ap(), CLAMP),
            e.tensor_mul(bv_t.ap(), bv_t.ap(), v_t.ap()),
        ])(), deps=[rden, rE_s], tag=f"p2c{s}")

        if dbg:
            sch.dma("sync", (lambda s=s: lambda e: e.dma_start(
                out=dbg_E.ap()[s], in_=E_t.ap()))(), deps=[esc2], n=1,
                tag=f"dbgE_{s}")
        lnal = sch.comp("scalar", (lambda: lambda e: e.activation(
            rE.c(0, NB * S), al_t.ap(), AF.Ln))(), deps=[p2c], tag=f"lnal{s}")

        def emit_p2d(e, s=s):
            out = []
            for b in range(NB):
                out.append(e.reduce_sum(C["Aseg"].c(b, b + 1),
                                        rE.c(b * S, (b + 1) * S), AX.X))
            out.extend(scan8w(
                e,
                lambda b: q_t.c(b * (S + 1) + 1, (b + 1) * (S + 1)),
                lambda b: al_t.c(b * S, (b + 1) * S),
                lambda b: bv_t.c(b * S, (b + 1) * S),
                (lambda b: 0.0) if s == 0 else (lambda b: C["qlast"].c(b, b + 1)),
                ALU.mult, ALU.add))
            out.append(e.tensor_copy(col8(C["qlast"]),
                                     strided_last(q_t, S + 1)))
            return out
        p2d = sch.comp("vector", emit_p2d, deps=[lnal, prev_accA],
                       tag=f"p2d{s}")
        accA = sch.comp("gpsimd", lambda e: e.tensor_add(
            C["Acore"].ap(), C["Acore"].ap(), C["Aseg"].ap()),
            deps=[p2d], tag=f"accA{s}")
        prev_accA = accA

        sp2 = sch.dma("sync", (lambda s=s: lambda e: [
            e.dma_start(out=sp["al"].ap()[s], in_=al_t.c(0, NB * S)),
            e.dma_start(out=sp["bv"].ap()[s], in_=bv_t.c(0, NB * S)),
            e.dma_start(out=sp["F"].ap()[s], in_=so_t.c(0, NB * S)),
            e.dma_start(out=sp["base"].ap()[s], in_=eku.c(0, NB * S)),
        ])(), deps=[p2c, p2d], n=4, tag=f"sp2_{s}")
        prev_sp2 = sp2
        prev_v2 = p2d

    ph2_done_v = prev_v2
    ph2_done = prev_sp2
    ph2_accA = prev_accA

    ag2_ld = sch.dma("gpsimd", lambda e: [
        e.dma_start(out=ag2i.ap()[:, 0:D].rearrange("a (b p) -> p a b", p=128),
                    in_=C["Acore"].ap().rearrange("p (a b) -> p a b", a=1),
                    allow_slow_non_contiguous=True),
        e.dma_start(out=ag2i.ap()[:, D:2 * D].rearrange("a (b p) -> p a b", p=128),
                    in_=C["qlast"].ap().rearrange("p (a b) -> p a b", a=1),
                    allow_slow_non_contiguous=True),
    ], deps=[ph2_done_v, ph2_accA], n=2, tag="ag2_ld")
    ag2 = sch.add("gpsimd", "cc", lambda e: e.collective_compute(
        "AllGather", ALU.bypass, replica_groups=[list(range(8))],
        ins=[ag2i.ap()], outs=[ag2o.ap()]), deps=[ag2_ld], tag="ag2")
    ag2_st = sch.dma("gpsimd", lambda e: [
        e.dma_start(
            out=T64["Atab"].ap().rearrange("p (b j) -> p b j", j=8)[:, :, j:j + 1],
            in_=ag2o.ap()[:, 0:D].rearrange("j (b p) -> p b j", p=128)[:, :, j:j + 1],
            allow_slow_non_contiguous=True)
        for j in range(8)] + [
        e.dma_start(
            out=T64["qtab"].ap().rearrange("p (b j) -> p b j", j=8)[:, :, j:j + 1],
            in_=ag2o.ap()[:, D:2 * D].rearrange("j (b p) -> p b j", p=128)[:, :, j:j + 1],
            allow_slow_non_contiguous=True)
        for j in range(8)], deps=[ag2], n=16, tag="ag2_st")
    ea_s = sch.comp("scalar", lambda e: e.activation(
        T64["EA"].ap(), T64["Atab"].ap(), AF.Exp), deps=[ag2_st], tag="ea")

    def emit_qfold(e):
        out = []
        Qt = T64["Qt"].ap().rearrange("p (b j) -> p b j", j=8)
        EA = T64["EA"].ap().rearrange("p (b j) -> p b j", j=8)
        qe = T64["qtab"].ap().rearrange("p (b j) -> p b j", j=8)
        out.append(fill(e, T64["Qt"].ap(), 0.0, T64["qtab"].ap()))
        for g in (0, 4):
            for j in range(g + 1, g + 4):
                out.append(e.tensor_mul(Qt[:, :, j:j + 1], EA[:, :, j - 1:j],
                                        Qt[:, :, j - 1:j]))
                out.append(e.tensor_add(Qt[:, :, j:j + 1], Qt[:, :, j:j + 1],
                                        qe[:, :, j - 1:j]))
        out.append(e.tensor_mul(T64["tmp"].ap(), T64["Qt"].ap(),
                                T64["sels"].ap()))
        t3 = T64["tmp"].ap().rearrange("p (b j) -> p b j", j=8)
        q3 = C["qin"].ap().rearrange("p (b j) -> p b j", j=1)
        out.append(e.tensor_copy(q3, t3[:, :, 0:1]))
        for j in range(1, 8):
            out.append(e.tensor_add(q3, q3, t3[:, :, j:j + 1]))
        return out
    qfold = sch.comp("gpsimd", emit_qfold, deps=[ea_s, ag2_st, ld_misc],
                     tag="qfold")

    for t in (rE, eku, EPt, so_t, al_t, bv_t, q_t, ek_t, v_t, r_t, E_t, ewt, canary):
        t.free()

    dbg_qf_step = None
    if dbg:
        dbg_qf_step = sch.dma("sync", lambda e: [
            e.dma_start(out=dbg_qf.ap()[0], in_=T64["qtab"].ap()),
            e.dma_start(out=dbg_qf.ap()[1], in_=T64["EA"].ap()),
            e.dma_start(out=dbg_qf.ap()[2], in_=T64["Qt"].ap()),
            e.dma_start(out=dbg_qf.ap()[3, :, 0:NB], in_=C["qin"].ap()),
            e.dma_start(out=dbg_qf.ap()[4, :, 0:NB], in_=C["Acore"].ap()),
            e.dma_start(out=dbg_qf.ap()[5], in_=T64["Atab"].ap()),
        ], deps=[qfold], n=6, tag="dbg_qf")
    for t in T64.values():
        t.free()
    prev_v3 = qfold
    prev_sp3 = ph2_done
    prev_wo_mm = None
    prev_ffn_ev = None
    prev_ffn_mm = None
    out_steps = []
    fsl = [T_(2048, f"fsl{i}") for i in range(2)]
    gs = [T_(FS, f"gs{i}") for i in range(2)]

    for s in range(nseg):
        BIG = NB * (S + 1)
        G6 = T_(6 * BIG, f"G6_{s}")
        names6 = ("al", "base", "r", "x")
        a3 = {nm: Tile(slab, SB, NB * S, f"{nm}3_{s}", off=G6.off + i * BIG)
              for i, nm in enumerate(names6)}
        for nm in names6:
            a3[nm].slab = None
        q3 = Tile(slab, SB, BIG, f"q3_{s}", off=G6.off + 4 * BIG)
        q3.slab = None
        a3["bv"] = T_(NB * S, f"bv3_{s}")
        a3["F"] = T_(NB * S, f"F3_{s}")
        a3["x2"] = T_(NB * S, f"x23_{s}")

        if s == 0 and dbg_qf_step is not None:
            prev_v3 = dbg_qf_step if False else prev_v3
        rld3 = sch.dma("sync", (lambda s=s, a3=a3, _d=dbg_qf_step: lambda e: [
            e.dma_start(out=a3["al"].c(0, NB * S), in_=sp["al"].ap()[s]),
            e.dma_start(out=a3["bv"].c(0, NB * S), in_=sp["bv"].ap()[s]),
            e.dma_start(out=a3["F"].c(0, NB * S), in_=sp["F"].ap()[s]),
            e.dma_start(out=a3["base"].c(0, NB * S), in_=sp["base"].ap()[s]),
            e.dma_start(out=a3["r"].c(0, NB * S), in_=sp["r"].ap()[s]),
            e.dma_start(out=a3["x"].c(0, NB * S).rearrange("p (b t) -> p b t", t=S),
                        in_=x_seg(s)),
        ])(), deps=[prev_sp3, prev_v3, dbg_qf_step if s == 0 else None],
            n=6, tag=f"p3_rld{s}")

        def emit_qtrue(e, s=s, a3=a3, q3=q3):
            out = []
            src = C["qin"] if s == 0 else C["qlast"]
            out.append(e.tensor_copy(bound_col(q3), col8(src)))
            out.extend(scan8w(
                e,
                lambda b: q3.c(b * (S + 1) + 1, (b + 1) * (S + 1)),
                lambda b: a3["al"].c(b * S, (b + 1) * S),
                lambda b: a3["bv"].c(b * S, (b + 1) * S),
                lambda b: src.c(b, b + 1), ALU.mult, ALU.add))
            out.append(e.tensor_copy(col8(C["qlast"]),
                                     strided_last(q3, S + 1)))
            for b in range(NB):
                out.append(e.tensor_mul(
                    a3["F"].c(b * S, (b + 1) * S),
                    a3["F"].c(b * S, (b + 1) * S),
                    q3.c(b * (S + 1), b * (S + 1) + S)))
            out.append(e.tensor_add(a3["F"].ap(), a3["F"].ap(),
                                    a3["base"].ap()))
            out.append(e.tensor_mul(a3["r"].ap(), a3["r"].ap(), a3["F"].ap()))
            out.extend(dek(e, a3["al"].c(0, NB * S), a3["bv"].c(0, NB * S),
                           a3["r"].c(0, NB * S)))
            return out
        qtrue = sch.comp("vector", emit_qtrue, deps=[rld3, qfold],
                         tag=f"qt{s}")

        wo_mm = attn_stream(f"pjo{s}", "o",
                            (lambda a3=a3: lambda db: a3["al"].r(db * S, (db + 1) * S))(),
                            (lambda a3=a3: lambda db: a3["bv"].r(db * S, (db + 1) * S))(),
                            qtrue, qtrue,
                            prev_ffn_mm if prev_ffn_mm else qtrue, slots=fsl)
        prev_wo_mm = wo_mm
        x2e = sch.comp("vector", (lambda a3=a3: lambda e: [
            e.tensor_add(a3["x2"].c(b * S, (b + 1) * S), pb[b][:, :],
                         a3["x"].c(b * S, (b + 1) * S)) for b in range(NB)
        ])(), deps=[wo_mm, qtrue], tag=f"x2e{s}")

        def pieces_ops(e, a3=a3):
            return dek(e, a3["bv"].c(0, NB * S), a3["F"].c(0, NB * S),
                       a3["al"].c(0, NB * S))
        norm2 = ln_emit("ln2", s, a3["x2"], a3["base"], a3["al"], x2e, x2e,
                        x2e, None, extra_norm_ops=pieces_ops)
        if dbg:
            sch.dma("sync", (lambda s=s, a3=a3: lambda e: [
                e.dma_start(out=dbg_x2.ap()[s], in_=a3["x2"].c(0, NB * S)),
                e.dma_start(out=dbg_fx.ap()[s], in_=a3["al"].c(0, NB * S)),
            ])(), deps=[norm2], n=2, tag=f"dbgx2_{s}")

        G6.free()
        spx2 = sch.dma("sync", (lambda s=s, a3=a3: lambda e: e.dma_start(
            out=outT.ap().rearrange("(b p) t -> p b t", p=128)
                [:, :, s * S:(s + 1) * S],
            in_=a3["x2"].c(0, NB * S).rearrange("p (b t) -> p b t", t=S)))(),
            deps=[norm2], n=1, tag=f"spx2_{s}")
        a3["x2"].free()

        PAh = T_(HB * FS, f"PAh_{s}")
        PAl = T_(HB * FS, f"PAl_{s}")
        PBh = T_(HB * FS, f"PBh_{s}")
        PBl = T_(HB * FS, f"PBl_{s}")

        fxh = a3["bv"]
        fxl = a3["F"]

        acc_steps = []
        for cch in range(2):
            t0 = cch * FS

            def ffn_sweep(tag, wT, n_out, n_in, rhs_h, rhs_l, evac_fn,
                          dep_rhs, tail_vec_fn=None, ret_ids=False):
                nonlocal prev_ffn_ev, prev_ffn_mm
                ndb = n_in // 128
                nhalf = max(1, ndb // 16)
                per = ndb // nhalf
                ev_ids = {}
                mm_last = {}
                for o in range(n_out):
                    for hf in range(nhalf):
                        key = o * nhalf + hf
                        par = key % 2
                        deps_ld = [ffn_prep_done_map[tag.split("_")[0]]]
                        if key < 2:
                            deps_ld.append(prev_ffn_mm)
                            if dbg and tag.startswith("h1c0_0"):
                                deps_ld.append(dbg_pa_step)
                        else:
                            deps_ld.append(mm_last[key - 2])
                        ld = sch.dma("sync", (lambda wT=wT, o=o, hf=hf,
                                              per=per, par=par: lambda e:
                            e.dma_start(
                                out=fsl[par].r(0, per * 128)
                                    .rearrange("p (d c) -> p d c", c=128),
                                in_=wT.ap()[o, hf * per:(hf + 1) * per]
                                    .rearrange("d p c -> p d c")))(),
                            deps=deps_ld, n=1, tag=f"{tag}_ld{o}_{hf}")
                        deps_mm = [ld] + dep_rhs
                        if o == 0 and hf == 0:
                            deps_mm.append(prev_ffn_ev)
                        if o >= 2 and hf == 0:
                            deps_mm.append(ev_ids[o - 2])
                        def emit_mms(e, o=o, hf=hf, per=per, par=par):
                            out = []
                            for d in range(per):
                                db = hf * per + d
                                wt = fsl[par].r(d * 128, (d + 1) * 128)
                                out.append(mm(e, pb[4 + (o % 2)][:, 0:FS], wt,
                                              rhs_h(db), db == 0, False))
                                out.append(mm(e, pb[4 + (o % 2)][:, 0:FS], wt,
                                              rhs_l(db), False, db == ndb - 1))
                            return out
                        mstep = sch.comp("tensor", emit_mms, deps=deps_mm,
                                         tag=f"{tag}_mm{o}_{hf}")
                        mm_last[key] = mstep
                    evd = [mm_last[o * nhalf + nhalf - 1]]
                    if o >= 2:
                        evd.append(ev_ids[o - 2])
                    ev = sch.comp("scalar", (lambda o=o: lambda e:
                                  evac_fn(e, o))(), deps=evd,
                                  tag=f"{tag}_ev{o}")
                    ev_ids[o] = ev
                    ev_ids[f"ev{o}"] = ev
                    if tail_vec_fn is not None:
                        tv = sch.comp("vector", (lambda o=o: lambda e:
                                      tail_vec_fn(e, o))(), deps=[ev],
                                      tag=f"{tag}_tv{o}")
                        ev_ids[o] = tv
                prev_ffn_ev = ev_ids[n_out - 1]
                prev_ffn_mm = mm_last[(n_out - 1) * nhalf + nhalf - 1]
                if ret_ids:
                    return ev_ids[n_out - 1], ev_ids
                return ev_ids[n_out - 1]

            ffn_prep_done_map = {f"w1c{cch}": ffn_prep_done["w1"],
                                 f"w2c{cch}": ffn_prep_done["w2"],
                                 f"h1c{cch}": ffn_prep_done["h1"],
                                 f"h2c{cch}": ffn_prep_done["h2"],
                                 f"w3c{cch}": ffn_prep_done["w3"]}

            w12_ids = {}
            for o in range(HB):
                par = o % 2
                deps_ld = [ffn_prep_done["w1"], ffn_prep_done["w2"]]
                if o < 2:
                    deps_ld.append(prev_ffn_mm)
                    deps_ld.append(wo_mm)
                else:
                    deps_ld.append(w12_ids[("mm2", o - 2)])
                ld = sch.dma("sync", (lambda o=o, par=par: lambda e: [
                    e.dma_start(out=fsl[par].r(0, D)
                                .rearrange("p (d c) -> p d c", c=128),
                                in_=wfT["w1"].ap()[o].rearrange("d p c -> p d c")),
                    e.dma_start(out=fsl[par].r(D, 2 * D)
                                .rearrange("p (d c) -> p d c", c=128),
                                in_=wfT["w2"].ap()[o].rearrange("d p c -> p d c")),
                ])(), deps=deps_ld, n=2, tag=f"w12c{cch}_{s}_ld{o}")
                if dbg and s == 0 and cch == 0 and o == 0:
                    sch.dma("sync", lambda e: e.dma_start(
                        out=dbg_wl.ap(), in_=fsl[0].c(0, 2 * D)),
                        deps=[ld], n=1, tag="dbg_wl")
                deps_m1 = [ld, norm2]
                if o == 0:
                    deps_m1.append(prev_ffn_ev)
                if o >= 2:
                    deps_m1.append(w12_ids[("tv", o - 2)])
                def emit_m1(e, o=o, par=par, t0=t0):
                    out = []
                    for db in range(NB):
                        wt = fsl[par].r(db * 128, (db + 1) * 128)
                        out.append(mm(e, pb[4 + 2 * (o % 2)][:, 0:FS], wt,
                                      fxh.r(db * S + t0, db * S + t0 + FS),
                                      db == 0, False))
                        out.append(mm(e, pb[4 + 2 * (o % 2)][:, 0:FS], wt,
                                      fxl.r(db * S + t0, db * S + t0 + FS),
                                      False, db == NB - 1))
                    for db in range(NB):
                        wt = fsl[par].r(D + db * 128, D + (db + 1) * 128)
                        out.append(mm(e, pb[5 + 2 * (o % 2)][:, 0:FS], wt,
                                      fxh.r(db * S + t0, db * S + t0 + FS),
                                      db == 0, False))
                        out.append(mm(e, pb[5 + 2 * (o % 2)][:, 0:FS], wt,
                                      fxl.r(db * S + t0, db * S + t0 + FS),
                                      False, db == NB - 1))
                    return out
                m1 = sch.comp("tensor", emit_m1, deps=deps_m1,
                              tag=f"w12c{cch}_{s}_mm{o}")
                w12_ids[("mm2", o)] = m1
                ev = sch.comp("scalar", (lambda o=o: lambda e: [
                    e.activation(gs[0].ap(),
                                 pb[4 + 2 * (o % 2)][:, 0:FS], AF.Silu,
                                 scale=SC["w1"].c(o, o + 1)),
                    e.activation(gs[1].ap(),
                                 pb[5 + 2 * (o % 2)][:, 0:FS], AF.Copy,
                                 scale=SC["w2"].c(o, o + 1)),
                ])(), deps=[m1] + ([w12_ids[("tv", o - 1)]] if o >= 1 else []),
                    tag=f"w12c{cch}_{s}_ev{o}")
                if dbg and s == 0 and cch == 0 and o == 0:
                    sch.dma("sync", lambda e: [
                        e.dma_start(out=dbg_gs.ap()[0], in_=gs[0].ap()),
                        e.dma_start(out=dbg_gs.ap()[1], in_=gs[1].ap()),
                    ], deps=[ev], n=2, tag="dbg_gs")
                    sch.dma("sync", (lambda fxh=fxh, fxl=fxl: lambda e: [
                        e.dma_start(out=dbg_fxp.ap()[0],
                                    in_=fxh.r(0, NB * S).bitcast(F32)),
                        e.dma_start(out=dbg_fxp.ap()[1],
                                    in_=fxl.r(0, NB * S).bitcast(F32)),
                    ])(), deps=[norm2], n=2, tag="dbg_fxp")
                tv_deps = [ev] + ([spx2] if (o == 0 and cch == 0) else [])
                tv = sch.comp("vector", (lambda o=o: lambda e: [
                    e.tensor_mul(gs[0].ap(), gs[0].ap(), gs[1].ap()),
                ] + dek(e, PAh.c(o * FS, (o + 1) * FS),
                        PAl.c(o * FS, (o + 1) * FS), gs[0].ap()))(),
                    deps=tv_deps, tag=f"w12c{cch}_{s}_tv{o}")
                w12_ids[("tv", o)] = tv
            prev_ffn_ev = w12_ids[("tv", HB - 1)]
            prev_ffn_mm = w12_ids[("mm2", HB - 1)]
            w12_done = w12_ids[("tv", HB - 1)]

            if dbg and s == 0 and cch == 0:
                sch.dma("sync", lambda e: [
                    e.dma_start(out=dbg_pa.ap()[0],
                                in_=PAh.r(0, HB * FS).bitcast(F32)),
                    e.dma_start(out=dbg_pa.ap()[1],
                                in_=PAl.r(0, HB * FS).bitcast(F32)),
                ], deps=[w12_done], n=2, tag="dbg_pa")
            def mk_h_evac(stage_sc, dstPh, dstPl):
                def evac(e, o):
                    return e.activation(gs[o % 2].ap(), pb[4 + (o % 2)][:, 0:FS],
                                        AF.Gelu, scale=stage_sc.c(o, o + 1))
                def tail(e, o):
                    return dek(e, dstPh.c(o * FS, (o + 1) * FS),
                               dstPl.c(o * FS, (o + 1) * FS), gs[o % 2].ap())
                return evac, tail

            dbg_pa_step = None
            if dbg and s == 0 and cch == 0:
                dbg_pa_step = [st for st in range(len(sch.steps))
                               if sch.steps[st]["tag"] == "dbg_pa"][0]
            evh, tlh = mk_h_evac(SC["h1"], PBh, PBl)
            h1ev = ffn_sweep(f"h1c{cch}_{s}", wfT["h1"], HB, H,
                             lambda db: PAh.r(db * FS, (db + 1) * FS),
                             lambda db: PAl.r(db * FS, (db + 1) * FS),
                             evh, [w12_done], tail_vec_fn=tlh)
            if dbg and s == 0 and cch == 0:
                sch.dma("sync", lambda e: [
                    e.dma_start(out=dbg_pb.ap()[0],
                                in_=PBh.r(0, HB * FS).bitcast(F32)),
                    e.dma_start(out=dbg_pb.ap()[1],
                                in_=PBl.r(0, HB * FS).bitcast(F32)),
                ], deps=[h1ev], n=2, tag="dbg_pb")
            evh2_, tlh2 = mk_h_evac(SC["h2"], PAh, PAl)
            h2ev = ffn_sweep(f"h2c{cch}_{s}", wfT["h2"], HB, H,
                             lambda db: PBh.r(db * FS, (db + 1) * FS),
                             lambda db: PBl.r(db * FS, (db + 1) * FS),
                             evh2_, [h1ev], tail_vec_fn=tlh2)

            def evw3(e, o):
                return e.activation(gs[o % 2].ap(), pb[4 + (o % 2)][:, 0:FS],
                                    AF.Copy, scale=C["w3s"].c(o, o + 1))
            w3ev, w3ids = ffn_sweep(f"w3c{cch}_{s}", wfT["w3"], NB, H,
                                    lambda db: PAh.r(db * FS, (db + 1) * FS),
                                    lambda db: PAl.r(db * FS, (db + 1) * FS),
                                    evw3, [h2ev], ret_ids=True)
            prev_acc2 = None
            for o in range(NB):
                acc = sch.dma("gpsimd", (lambda s=s, o=o, t0=t0: lambda e:
                    e.dma_start(
                        out=outT.ap().rearrange("(b p) t -> p b t", p=128)
                            [:, o:o + 1, s * S + t0:s * S + t0 + FS],
                        in_=gs[o % 2].ap().rearrange("p (a t) -> p a t", a=1),
                        accum_op=ALU.add))(),
                    deps=[w3ids[o], spx2, prev_acc2], n=1,
                    tag=f"acc3c{cch}_{s}_{o}")
                if o + 2 < NB:
                    sch.dep(w3ids[f"ev{o+2}"], acc)
                prev_acc2 = acc
                acc_steps.append(acc)
            prev_ffn_ev = acc if False else prev_ffn_ev

        out_steps.extend(acc_steps)
        prev_v3 = prev_ffn_ev
        prev_sp3 = acc_steps[-1]

        PAh.free()
        PAl.free()
        PBh.free()
        PBl.free()
        fxh.free()
        fxl.free()

    fin = sch.comp("gpsimd", lambda e: e.engine_nop(), deps=out_steps,
                   tag="final")

    sch.assign_counts()
    from contextlib import ExitStack
    stack = ExitStack()
    sems = {}
    for k in Sched.ENGS:
        sems[k] = stack.enter_context(nc.semaphore(f"s_{k[0]}_{k[1]}"))
    block = stack.enter_context(nc.Block())
    with stack:
        @block.sync
        def _(eng):
            sch.emit_engine("sync", eng, sems, emit_limit)

        @block.gpsimd
        def _(eng):
            sch.emit_engine("gpsimd", eng, sems, emit_limit)

        @block.scalar
        def _(eng):
            sch.emit_engine("scalar", eng, sems, emit_limit)

        @block.vector
        def _(eng):
            sch.emit_engine("vector", eng, sems, emit_limit)

        @block.tensor
        def _(eng):
            sch.emit_engine("tensor", eng, sems, emit_limit)

    n_inst = sum(1 for _ in ())
    print(f"[rk] nseg={nseg} steps={len(sch.steps)} slab_peak={slab.peak}/{TOT}")
    build_nc.tags = {st["tag"]: i for i, st in enumerate(sch.steps)}
    build_nc.allocs_log = allocs_log
    return nc


def make_in_maps(inputs, nseg=8):
    L = nseg * S
    x = np.asarray(inputs["L0_attn_x"])
    B, T, _ = x.shape
    assert T == 4 * L, (T, L)
    xt = np.ascontiguousarray(
        x.reshape(B * T, D).T)
    def vec_cols(v):
        return np.ascontiguousarray(np.asarray(v).reshape(NB, 128).T)
    def vec_cols32(v):
        return np.ascontiguousarray(np.asarray(v).reshape(HB, 128).T)
    common = {
        "Wk": np.ascontiguousarray(inputs["L0_attn_W_k"]),
        "Wv": np.ascontiguousarray(inputs["L0_attn_W_v"]),
        "Wr": np.ascontiguousarray(inputs["L0_attn_W_r"]),
        "Wo": np.ascontiguousarray(inputs["L0_attn_W_o"]),
        "tmk": vec_cols(inputs["L0_attn_time_mix_k"]),
        "tmv": vec_cols(inputs["L0_attn_time_mix_v"]),
        "tmr": vec_cols(inputs["L0_attn_time_mix_r"]),
        "tdec": vec_cols(inputs["L0_attn_time_decay"]),
        "tfir": vec_cols(inputs["L0_attn_time_first"]),
        "w1": np.ascontiguousarray(inputs["L0_ffn_w1_w"]),
        "w2": np.ascontiguousarray(inputs["L0_ffn_w2_w"]),
        "h1": np.ascontiguousarray(inputs["L0_ffn_h1_w"]),
        "h2": np.ascontiguousarray(inputs["L0_ffn_h2_w"]),
        "w3": np.ascontiguousarray(inputs["L0_ffn_w3_w"]),
        "w1s": vec_cols32(inputs["L0_ffn_w1_s"]),
        "w2s": vec_cols32(inputs["L0_ffn_w2_s"]),
        "h1s": vec_cols32(inputs["L0_ffn_h1_s"]),
        "h2s": vec_cols32(inputs["L0_ffn_h2_s"]),
        "w3s": vec_cols(inputs["L0_ffn_w3_s"]),
    }
    in_maps = []
    for c in range(8):
        b = c // 4
        j = c % 4
        t0 = j * L
        xs = np.zeros((D, 1 + L), np.float32)
        xs[:, 1:] = xt[:, b * T + t0: b * T + t0 + L]
        if j > 0:
            xs[:, 0] = xt[:, b * T + t0 - 1]
        sel_p = np.zeros(8, np.float32)
        if j > 0:
            sel_p[c - 1] = 1.0
        sel_s = np.zeros(8, np.float32)
        sel_s[c] = 1.0
        m = dict(common)
        m["xT"] = xs
        m["selp"] = np.ascontiguousarray(np.tile(sel_p, (128, 8)))
        m["sels"] = np.ascontiguousarray(np.tile(sel_s, (128, 8)))
        in_maps.append(m)
    return in_maps


def gather_out(results, nseg=8):
    L = nseg * S
    B, T = 2, 4 * L
    out = np.empty((B, T, D), np.float32)
    for c in range(8):
        b = c // 4
        j = c % 4
        o = np.asarray(results[c]["outT"])
        out[b, j * L:(j + 1) * L, :] = o.T
    return out


from concourse.bass_utils import run_bass_kernel_spmd

_NC = None


def kernel(**inputs):
    global _NC
    if _NC is None:
        _NC = build_nc(nseg=8, dbg=False)
    in_maps = make_in_maps(inputs, nseg=8)
    res = run_bass_kernel_spmd(_NC, in_maps, core_ids=list(range(8)))
    return gather_out(res.results, nseg=8)
